# version 04 (final)
# baseline (speedup 1.0000x reference)
import numpy as np
import ml_dtypes
import concourse.bass as bass
import concourse.mybir as mybir
from concourse.bass_utils import run_bass_kernel_spmd

F32 = mybir.dt.float32
BF16 = mybir.dt.bfloat16
AF = mybir.ActivationFunctionType
OP = mybir.AluOpType
AX = mybir.AxisListType

NC = 8
MROWS = 2048
NEGBIG = -3.4028235e38

BF = ml_dtypes.bfloat16


class Prog:

    def __init__(self):
        self.nc = bass.Bass("TRN2", target_bir_lowering=False)
        self.items = []
        self._ctxs = []
        self._done = False

    def din(self, name, shape, dt=F32):
        return self.nc.dram_tensor(name, list(shape), dt, kind="ExternalInput").ap()

    def dout(self, name, shape, dt=F32):
        return self.nc.dram_tensor(name, list(shape), dt, kind="ExternalOutput").ap()

    def sb(self, name, shape, dt=F32):
        cm = self.nc.sbuf_tensor(name, list(shape), dt)
        t = cm.__enter__()
        self._ctxs.append(cm)
        return t

    def ps(self, name, shape, dt=F32):
        cm = self.nc.psum_tensor(name, list(shape), dt)
        t = cm.__enter__()
        self._ctxs.append(cm)
        return t

    def _add(self, eng, fn, dma=False):
        self.items.append((eng, fn, dma))

    @staticmethod
    def _ap(v):
        if v is None or isinstance(v, (int, float, bass.AP)):
            return v
        return v[:]

    def dma(self, dst, src):
        dst = self._ap(dst); src = self._ap(src)
        self._add("sync", lambda e, d=dst, s=src: e.dma_start(out=d, in_=s), True)

    def mm(self, out, lhsT, rhs, start=True, stop=True):
        out = self._ap(out); lhsT = self._ap(lhsT); rhs = self._ap(rhs)
        self._add("tensor",
                  lambda e, o=out, l=lhsT, r=rhs, st=start, sp=stop:
                  self.nc.tensor.matmul(o, l, r, start=st, stop=sp))

    def tr(self, out, in_, ident):
        out = self._ap(out); in_ = self._ap(in_); ident = self._ap(ident)
        self._add("tensor",
                  lambda e, o=out, i=in_, d=ident: self.nc.tensor.transpose(o, i, d))

    def tt(self, op, out, a, b):
        out = self._ap(out); a = self._ap(a); b = self._ap(b)
        self._add("vector",
                  lambda e, o=out, x=a, y=b, p=op: self.nc.vector.tensor_tensor(o, x, y, p))

    def ts(self, out, a, s1, op0, s2=None, op1=None):
        out = self._ap(out); a = self._ap(a); s1 = self._ap(s1); s2 = self._ap(s2)
        if op1 is None:
            self._add("vector",
                      lambda e, o=out, x=a, s=s1, p=op0:
                      self.nc.vector.tensor_single_scalar(o, x, s, p))
        else:
            self._add("vector",
                      lambda e, o=out, x=a, sa=s1, sb_=s2, p0=op0, p1=op1:
                      self.nc.vector.tensor_scalar(o, x, sa, sb_, p0, p1))

    def stt(self, out, in0, scalar, in1, op0, op1):
        out = self._ap(out); in0 = self._ap(in0)
        scalar = self._ap(scalar); in1 = self._ap(in1)
        self._add("vector",
                  lambda e, o=out, a=in0, s=scalar, b=in1, p0=op0, p1=op1:
                  self.nc.vector.scalar_tensor_tensor(o, a, s, b, p0, p1))

    def red(self, out, in_, op):
        out = self._ap(out); in_ = self._ap(in_)
        self._add("vector",
                  lambda e, o=out, i=in_, p=op: self.nc.vector.tensor_reduce(o, i, AX.X, p))

    def act(self, out, in_, func, bias=0.0, scale=1.0):
        out = self._ap(out); in_ = self._ap(in_)
        self._add("scalar",
                  lambda e, o=out, i=in_, f=func, b=bias, s=scale:
                  self.nc.scalar.activation(o, i, f, bias=b, scale=s))

    def cp(self, out, in_):
        out = self._ap(out); in_ = self._ap(in_)
        self._add("vector", lambda e, o=out, i=in_: self.nc.vector.tensor_copy(o, i))

    def recip(self, out, in_):
        out = self._ap(out); in_ = self._ap(in_)
        self._add("vector", lambda e, o=out, i=in_: self.nc.vector.reciprocal(o, i))

    def mset(self, ap, v):
        ap = self._ap(ap)
        self._add("vector", lambda e, a=ap, vv=v: self.nc.vector.memset(a, vv))

    def finalize(self):
        assert not self._done
        self._done = True
        nc = self.nc
        waits = []
        c = 0
        for (_, _, dma) in self.items:
            waits.append(c)
            c += 16 if dma else 1
        total = c
        sem_cm = nc.semaphore("S")
        sem = sem_cm.__enter__()
        self._ctxs.append(sem_cm)
        blk_cm = nc.Block()
        block = blk_cm.__enter__()

        def mk(eng_name, engobj):
            for i, (e, fn, dma) in enumerate(self.items):
                if e != eng_name:
                    continue
                engobj.wait_ge(sem, waits[i])
                inst = fn(engobj)
                inst.then_inc(sem, 16 if dma else 1)
            engobj.wait_ge(sem, total)

        @block.sync
        def _(e):
            mk("sync", e)

        @block.tensor
        def _(e):
            mk("tensor", e)

        @block.vector
        def _(e):
            mk("vector", e)

        @block.scalar
        def _(e):
            mk("scalar", e)

        blk_cm.__exit__(None, None, None)
        for cm in reversed(self._ctxs):
            try:
                cm.__exit__(None, None, None)
            except Exception:
                pass
        return nc


_cache = {}


def get_prog(key, builder):
    if key not in _cache:
        p = Prog()
        builder(p)
        _cache[key] = p.finalize()
    return _cache[key]


def launch(key, builder, inmaps, outnames):
    nc = get_prog(key, builder)
    res = run_bass_kernel_spmd(nc, inmaps, core_ids=list(range(NC)))
    return [[np.asarray(r[n]) for n in outnames] for r in res.results]



def emit_bcast_row(p, row_ap, D, dst_sb, ones1, pbank):
    c0 = 0
    while c0 < D:
        w = min(512, D - c0)
        p.mm(pbank[:, 0:w], ones1[:, :], row_ap[:, c0:c0 + w])
        p.cp(dst_sb[:, c0:c0 + w], pbank[:, 0:w])
        c0 += w


def emit_ln(p, xt, D, mu, sq, var, eps, out, sub_mean=True):
    if sub_mean:
        p.red(mu, xt[:, 0:D], OP.add)
        p.ts(mu, mu, -1.0 / D, OP.mult)
        p.ts(out, xt[:, 0:D], mu, OP.add)
        src = out
    else:
        src = xt
    p.act(sq, src[:, 0:D], AF.Square)
    p.red(var, sq[:, 0:D], OP.add)
    p.ts(var, var, 1.0 / D, OP.mult, eps, OP.add)
    p.act(var, var, AF.Sqrt)
    p.recip(var, var)
    p.ts(out, src[:, 0:D], var, OP.mult)


def emit_load_wbf(p, W_hbm, K, N, W_bf, wtmp):
    nkt = K // 128
    for kt in range(nkt):
        c0 = 0
        while c0 < N:
            w = min(512, N - c0)
            p.dma(wtmp[:, 0:w], W_hbm[kt * 128:(kt + 1) * 128, c0:c0 + w])
            p.cp(W_bf[:, kt * N + c0: kt * N + c0 + w], wtmp[:, 0:w])
            c0 += w


def emit_xT(p, xb, K, xT, ident_bf, pT):
    for kt in range(K // 128):
        p.tr(pT[:, 0:128], xb[:, kt * 128:(kt + 1) * 128], ident_bf[:, :])
        p.cp(xT[:, kt * 128:(kt + 1) * 128], pT[:, 0:128])


def emit_matmul_tiles(p, xT, W_bf, K, N, out_sb, pbank):
    nkt = K // 128
    c0 = 0
    while c0 < N:
        w = min(512, N - c0)
        for kt in range(nkt):
            p.mm(pbank[:, 0:w], xT[:, kt * 128:(kt + 1) * 128],
                 W_bf[:, kt * N + c0: kt * N + c0 + w],
                 start=(kt == 0), stop=(kt == nkt - 1))
        p.cp(out_sb[:, c0:c0 + w], pbank[:, 0:w])
        c0 += w



def build_linear(cfg):
    K, N = cfg["K"], cfg["N"]
    pro, epi = cfg["pro"], cfg["epi"]
    Nout = cfg.get("Nout", N)

    def b(p):
        x = p.din("x", [MROWS, K])
        W = p.din("W", [K, N])
        y = p.dout("y", [MROWS, Nout])
        ident_bf = p.din("identbf", [128, 128], BF16)
        ones1 = p.din("ones1", [1, 128])

        W_bf = p.sb("W_bf", [128, (K // 128) * N], BF16)
        wtmp = p.sb("wtmp", [128, 512])
        xt = p.sb("xt", [128, K])
        xb = p.sb("xb", [128, K], BF16)
        xT = p.sb("xT", [128, K], BF16)
        out_sb = p.sb("out_sb", [128, N])
        if pro != "none":
            mu = p.sb("mu", [128, 1])
            var = p.sb("var", [128, 1])
            sq = p.sb("sq", [128, K])
        pbank = p.ps("pbank", [128, 512])
        pT = p.ps("pT", [128, 128], BF16)

        idsb = p.sb("idsb", [128, 128], BF16)
        p.dma(idsb[:, :], ident_bf[:, :])
        onesb = p.sb("onesb", [1, 128])
        p.dma(onesb[:, :], ones1[:, :])

        if pro == "rms_w":
            wrow = p.din("prow", [1, K])
            wr_sb = p.sb("wr_sb", [1, K])
            p.dma(wr_sb[:, :], wrow[:, :])
            wbc = p.sb("wbc", [128, K])
            emit_bcast_row(p, wr_sb, K, wbc, onesb, pbank)

        if epi in ("res", "scale_res"):
            res = p.din("res", [MROWS, Nout])
            rt = p.sb("rt", [128, Nout])
        if epi == "mask_acc":
            res = p.din("res", [MROWS, Nout])
            rt = p.sb("rt", [128, Nout])
            mcol = p.din("mcol", [MROWS, 1])
            mt_sb = p.sb("mt_sb", [128, 1])
        if epi in ("scale_gelu", "scale_res"):
            srow = p.din("srow", [1, N])
            sr_sb = p.sb("sr_sb", [1, N])
            p.dma(sr_sb[:, :], srow[:, :])
            sbc = p.sb("sbc", [128, N])
            emit_bcast_row(p, sr_sb, N, sbc, onesb, pbank)
        if epi == "gate_swiglu":
            s1 = p.din("s1row", [1, N // 2])
            s2 = p.din("s2row", [1, N // 2])
            s1_sb = p.sb("s1_sb", [1, N // 2])
            s2_sb = p.sb("s2_sb", [1, N // 2])
            p.dma(s1_sb[:, :], s1[:, :])
            p.dma(s2_sb[:, :], s2[:, :])
            s1bc = p.sb("s1bc", [128, N // 2])
            s2bc = p.sb("s2bc", [128, N // 2])
            emit_bcast_row(p, s1_sb, N // 2, s1bc, onesb, pbank)
            emit_bcast_row(p, s2_sb, N // 2, s2bc, onesb, pbank)
            gt = p.sb("gt", [128, N // 2])
        if epi == "moe_gate":
            gt = p.sb("gt", [128, N // 2])
        if epi == "rope":
            cosi = p.din("cost", [MROWS, 64])
            sini = p.din("sint", [MROWS, 64])
            cos_t = p.sb("cos_t", [128, 64])
            sin_t = p.sb("sin_t", [128, 64])
            rtmp = p.sb("rtmp", [128, 64])
            r2 = p.sb("r2", [128, 64])

        emit_load_wbf(p, W, K, N, W_bf, wtmp)

        for mt in range(MROWS // 128):
            r0 = mt * 128
            p.dma(xt[:, :], x[r0:r0 + 128, :])
            if pro == "ln":
                emit_ln(p, xt, K, mu, sq, var, 1e-5, xt, True)
            elif pro == "rms_w":
                emit_ln(p, xt, K, mu, sq, var, 1e-6, xt, False)
                p.tt(OP.mult, xt[:, :], xt[:, :], wbc[:, :])
            p.cp(xb[:, :], xt[:, :])
            emit_xT(p, xb, K, xT, idsb, pT)
            emit_matmul_tiles(p, xT, W_bf, K, N, out_sb, pbank)

            if epi == "none":
                p.dma(y[r0:r0 + 128, :], out_sb[:, 0:N])
            elif epi == "res":
                p.dma(rt[:, :], res[r0:r0 + 128, :])
                p.tt(OP.add, out_sb[:, 0:N], out_sb[:, 0:N], rt[:, :])
                p.dma(y[r0:r0 + 128, :], out_sb[:, 0:N])
            elif epi == "scale_res":
                p.dma(rt[:, :], res[r0:r0 + 128, :])
                p.tt(OP.mult, out_sb[:, 0:N], out_sb[:, 0:N], sbc[:, :])
                p.tt(OP.add, out_sb[:, 0:N], out_sb[:, 0:N], rt[:, :])
                p.dma(y[r0:r0 + 128, :], out_sb[:, 0:N])
            elif epi == "scale_gelu":
                p.tt(OP.mult, out_sb[:, 0:N], out_sb[:, 0:N], sbc[:, :])
                p.act(out_sb[:, 0:N], out_sb[:, 0:N], AF.Gelu)
                p.dma(y[r0:r0 + 128, :], out_sb[:, 0:N])
            elif epi == "gate_swiglu":
                H = N // 2
                p.tt(OP.mult, out_sb[:, 0:H], out_sb[:, 0:H], s1bc[:, :])
                p.tt(OP.mult, out_sb[:, H:N], out_sb[:, H:N], s2bc[:, :])
                p.act(gt[:, :], out_sb[:, 0:H], AF.Silu)
                p.tt(OP.mult, gt[:, :], gt[:, :], out_sb[:, H:N])
                p.dma(y[r0:r0 + 128, :], gt[:, :])
            elif epi == "moe_gate":
                H = N // 2
                p.act(gt[:, :], out_sb[:, 0:H], AF.Sigmoid)
                p.tt(OP.mult, gt[:, :], gt[:, :], out_sb[:, 0:H])
                p.tt(OP.mult, gt[:, :], gt[:, :], out_sb[:, H:N])
                p.dma(y[r0:r0 + 128, :], gt[:, :])
            elif epi == "mask_acc":
                p.dma(rt[:, :], res[r0:r0 + 128, :])
                p.dma(mt_sb[:, :], mcol[r0:r0 + 128, :])
                p.stt(out_sb[:, 0:N], out_sb[:, 0:N], mt_sb[:, 0:1], rt[:, :],
                      OP.mult, OP.add)
                p.dma(y[r0:r0 + 128, :], out_sb[:, 0:N])
            elif epi == "rope":
                p.dma(cos_t[:, :], cosi[r0:r0 + 128, :])
                p.dma(sin_t[:, :], sini[r0:r0 + 128, :])
                for part in range(2):
                    for h in range(16):
                        base = part * 1024 + h * 64
                        sl = out_sb[:, base:base + 64]
                        p.tt(OP.mult, rtmp[:, 0:32], sl[:, 32:64], sin_t[:, 0:32])
                        p.ts(rtmp[:, 0:32], rtmp[:, 0:32], -1.0, OP.mult)
                        p.tt(OP.mult, rtmp[:, 32:64], sl[:, 0:32], sin_t[:, 32:64])
                        p.tt(OP.mult, r2[:, :], sl[:, :], cos_t[:, :])
                        p.tt(OP.add, sl[:, :], r2[:, :], rtmp[:, :])
                p.dma(y[r0:r0 + 128, :], out_sb[:, 0:N])
        _ = y

    return b



def build_ln_sb(p):
    D = 1024
    x = p.din("x", [MROWS, D])
    wrow = p.din("wrow", [1, D])
    brow = p.din("brow", [1, D])
    ones1 = p.din("ones1", [1, 128])
    y = p.dout("y", [MROWS, D])
    xt = p.sb("xt", [128, D])
    mu = p.sb("mu", [128, 1])
    var = p.sb("var", [128, 1])
    sq = p.sb("sq", [128, D])
    pbank = p.ps("pbank", [128, 512])
    onesb = p.sb("onesb", [1, 128])
    p.dma(onesb[:, :], ones1[:, :])
    wr = p.sb("wr", [1, D])
    br = p.sb("br", [1, D])
    p.dma(wr[:, :], wrow[:, :])
    p.dma(br[:, :], brow[:, :])
    wbc = p.sb("wbc", [128, D])
    bbc = p.sb("bbc", [128, D])
    emit_bcast_row(p, wr, D, wbc, onesb, pbank)
    emit_bcast_row(p, br, D, bbc, onesb, pbank)
    for mt in range(MROWS // 128):
        r0 = mt * 128
        p.dma(xt[:, :], x[r0:r0 + 128, :])
        emit_ln(p, xt, D, mu, sq, var, 1e-5, xt, True)
        p.tt(OP.mult, xt[:, :], xt[:, :], wbc[:, :])
        p.tt(OP.add, xt[:, :], xt[:, :], bbc[:, :])
        p.dma(y[r0:r0 + 128, :], xt[:, :])



def build_p1b(p):
    CW = 4640
    x = p.din("x", [MROWS + 3, CW])
    cwr = p.din("cwr", [1, 4 * 3072])
    qn = p.din("qn", [1, 64])
    kn = p.din("kn", [1, 64])
    onw_dummy = None
    gs = p.din("gs", [1, 16])
    bfr = p.din("bfr", [1, 16])
    blr = p.din("blr", [1, 16])
    ones1 = p.din("ones1", [1, 128])

    ktil = p.dout("ktil", [MROWS, 1024])
    qtil = p.dout("qtil", [MROWS, 1024])
    vout = p.dout("vout", [MROWS, 1024])
    aout = p.dout("aout", [MROWS, 16])
    bout = p.dout("bout", [MROWS, 16])

    onesb = p.sb("onesb", [1, 128])
    p.dma(onesb[:, :], ones1[:, :])
    pbank = p.ps("pbank", [128, 512])

    cw_sb = p.sb("cw_sb", [1, 4 * 3072])
    p.dma(cw_sb[:, :], cwr[:, :])
    cwb = p.sb("cwb", [128, 4 * 3072])
    for j in range(4):
        emit_bcast_row(p, cw_sb[0:1, j * 3072:(j + 1) * 3072], 3072,
                       cwb[:, j * 3072:(j + 1) * 3072], onesb, pbank)

    qn_sb = p.sb("qn_sb", [1, 64])
    kn_sb = p.sb("kn_sb", [1, 64])
    p.dma(qn_sb[:, :], qn[:, :])
    p.dma(kn_sb[:, :], kn[:, :])
    qnb = p.sb("qnb", [128, 64])
    knb = p.sb("knb", [128, 64])
    emit_bcast_row(p, qn_sb, 64, qnb, onesb, pbank)
    emit_bcast_row(p, kn_sb, 64, knb, onesb, pbank)

    gs_sb = p.sb("gs_sb", [1, 16])
    p.dma(gs_sb[:, :], gs[:, :])
    p.act(gs_sb[:, :], gs_sb[:, :], AF.Abs)
    lamb = p.sb("lamb", [128, 16])
    emit_bcast_row(p, gs_sb, 16, lamb, onesb, pbank)
    bf_sb = p.sb("bf_sb", [1, 16])
    bl_sb = p.sb("bl_sb", [1, 16])
    p.dma(bf_sb[:, :], bfr[:, :])
    p.dma(bl_sb[:, :], blr[:, :])
    bfb = p.sb("bfb", [128, 16])
    blb = p.sb("blb", [128, 16])
    emit_bcast_row(p, bf_sb, 16, bfb, onesb, pbank)
    emit_bcast_row(p, bl_sb, 16, blb, onesb, pbank)

    xs = p.sb("xs", [128, 3072])
    acc = p.sb("acc", [128, 3072])
    tmp = p.sb("tmp", [128, 3072])
    rest = p.sb("rest", [128, 1568])
    ss = p.sb("ss", [128, 16])
    rr = p.sb("rr", [128, 16])
    sq64 = p.sb("sq64", [128, 64])
    kt_sb = p.sb("kt_sb", [128, 1024])
    qt_sb = p.sb("qt_sb", [128, 1024])
    g16 = p.sb("g16", [128, 16])

    for mt in range(MROWS // 128):
        r0 = mt * 128
        for j in range(4):
            p.dma(xs[:, :], x[r0 + j: r0 + j + 128, 0:3072])
            if j == 0:
                p.tt(OP.mult, acc[:, :], xs[:, :], cwb[:, 0:3072])
            else:
                p.tt(OP.mult, tmp[:, :], xs[:, :], cwb[:, j * 3072:(j + 1) * 3072])
                p.tt(OP.add, acc[:, :], acc[:, :], tmp[:, :])
        p.dma(rest[:, :], x[r0 + 3: r0 + 131, 3072:4640])

        for (off, nb, dst) in ((0, qnb, qt_sb), (1024, knb, kt_sb)):
            for h in range(16):
                sl = acc[:, off + h * 64: off + (h + 1) * 64]
                p.act(sq64[:, :], sl, AF.Square)
                p.red(ss[:, h:h + 1], sq64[:, :], OP.add)
            p.ts(ss[:, :], ss[:, :], 1.0 / 64, OP.mult, 1e-6, OP.add)
            p.act(ss[:, :], ss[:, :], AF.Sqrt)
            p.recip(rr[:, :], ss[:, :])
            for h in range(16):
                sl = acc[:, off + h * 64: off + (h + 1) * 64]
                d = dst[:, h * 64:(h + 1) * 64]
                p.ts(d, sl, rr[:, h:h + 1], OP.mult)
                p.tt(OP.mult, d, d, nb[:, :])

        for h in range(16):
            rq = rest[:, h * 48:(h + 1) * 48]
            rk = rest[:, 768 + h * 48: 768 + (h + 1) * 48]
            p.stt(qt_sb[:, h * 64: h * 64 + 48], rq, lamb[:, h:h + 1],
                  qt_sb[:, h * 64: h * 64 + 48], OP.mult, OP.add)
            p.stt(kt_sb[:, h * 64: h * 64 + 48], rk, lamb[:, h:h + 1],
                  kt_sb[:, h * 64: h * 64 + 48], OP.mult, OP.add)
        for h in range(16):
            sl = kt_sb[:, h * 64:(h + 1) * 64]
            p.act(sq64[:, :], sl, AF.Square)
            p.red(ss[:, h:h + 1], sq64[:, :], OP.add)
        p.act(ss[:, :], ss[:, :], AF.Sqrt)
        p.ts(ss[:, :], ss[:, :], 1e-12, OP.max)
        p.recip(rr[:, :], ss[:, :])
        for h in range(16):
            sl = kt_sb[:, h * 64:(h + 1) * 64]
            p.ts(sl, sl, rr[:, h:h + 1], OP.mult)

        p.tt(OP.add, g16[:, :], rest[:, 1536:1552], bfb[:, :])
        p.act(g16[:, :], g16[:, :], AF.Sigmoid)
        p.dma(aout[r0:r0 + 128, :], g16[:, :])
        p.tt(OP.add, g16[:, :], rest[:, 1552:1568], blb[:, :])
        p.act(g16[:, :], g16[:, :], AF.Sigmoid)
        p.dma(bout[r0:r0 + 128, :], g16[:, :])

        p.dma(ktil[r0:r0 + 128, :], kt_sb[:, :])
        p.dma(qtil[r0:r0 + 128, :], qt_sb[:, :])
        p.dma(vout[r0:r0 + 128, :], acc[:, 2048:3072])



SEG = 512
SCH = 128


def build_scan(p):
    k_in = p.din("k", [4, SEG, 64])
    v_in = p.din("v", [4, SEG, 64])
    q_in = p.din("q", [4, SEG, 64])
    al = [p.din("al0", [2, SEG]), p.din("al1", [2, SEG])]
    be = [p.din("be0", [2, SEG]), p.din("be1", [2, SEG])]
    mem_in = [p.din("mem0", [128, 64]), p.din("mem1", [128, 64])]
    indic_i = p.din("indic", [2, 128])
    ident = p.din("ident", [128, 128])

    o_out = p.dout("o", [4, SEG, 64])
    mem_out = [p.dout("memo0", [128, 64]), p.dout("memo1", [128, 64])]

    idsb = p.sb("idsb", [128, 128])
    p.dma(idsb[:, :], ident[:, :])
    indic = p.sb("indic_sb", [2, 128])
    p.dma(indic[:, :], indic_i[:, :])

    memT = [p.sb("memT0", [128, 64]), p.sb("memT1", [128, 64])]
    al_sb = [p.sb("al_sb0", [2, SEG]), p.sb("al_sb1", [2, SEG])]
    be_sb = [p.sb("be_sb0", [2, SEG]), p.sb("be_sb1", [2, SEG])]
    for g in range(2):
        p.dma(memT[g][:, :], mem_in[g][:, :])
        p.dma(al_sb[g][:, :], al[g][:, :])
        p.dma(be_sb[g][:, :], be[g][:, :])

    kcol = [p.sb("kcol0", [128, 2, SCH]), p.sb("kcol1", [128, 2, SCH])]
    kbcol = [p.sb("kbcol0", [128, 2, SCH]), p.sb("kbcol1", [128, 2, SCH])]
    qcol = [p.sb("qcol0", [128, 2, SCH]), p.sb("qcol1", [128, 2, SCH])]
    kstack = [p.sb("kst0", [128, SCH]), p.sb("kst1", [128, SCH])]
    arep = [p.sb("arep0", [128, SCH]), p.sb("arep1", [128, SCH])]
    brep = [p.sb("brep0", [128, SCH]), p.sb("brep1", [128, SCH])]
    vrows = [p.sb("vr0", [2, SCH * 64]), p.sb("vr1", [2, SCH * 64])]
    orows = [p.sb("or0", [2, SCH * 64]), p.sb("or1", [2, SCH * 64])]
    krows = p.sb("krows", [128, 128])
    urow = [p.sb("urow0", [2, 64]), p.sb("urow1", [2, 64])]

    pTr = p.ps("pTr", [128, 128])
    pR = p.ps("pR", [2, 64])
    pU = p.ps("pU", [128, 64])
    pO = p.ps("pO", [2, 64])
    pB = p.ps("pB", [128, SCH])

    nch = SEG // SCH
    for ch in range(nch):
        c0 = ch * SCH
        for g in range(2):
            p.dma(vrows[g][:, :], v_in[2 * g:2 * g + 2, c0:c0 + SCH, :])
            p.mset(kcol[g][:, :, :], 0.0)
            p.mset(qcol[g][:, :, :], 0.0)
            for g1 in range(2):
                bh = 2 * g + g1
                lo = 64 * g1
                p.dma(krows[:, lo:lo + 64], k_in[bh, c0:c0 + SCH, :])
                p.tr(pTr[:, 0:SCH], krows[:, :], idsb[:, :])
                p.cp(kcol[g][lo:lo + 64, g1, :], pTr[lo:lo + 64, 0:SCH])
                p.dma(krows[:, lo:lo + 64], q_in[bh, c0:c0 + SCH, :])
                p.tr(pTr[:, 0:SCH], krows[:, :], idsb[:, :])
                p.cp(qcol[g][lo:lo + 64, g1, :], pTr[lo:lo + 64, 0:SCH])
            p.tt(OP.add, kstack[g][:, :], kcol[g][:, 0, :], kcol[g][:, 1, :])
            p.mm(pB[:, 0:SCH], indic[:, :], al_sb[g][:, c0:c0 + SCH])
            p.cp(arep[g][:, :], pB[:, 0:SCH])
            p.mm(pB[:, 0:SCH], indic[:, :], be_sb[g][:, c0:c0 + SCH])
            p.cp(brep[g][:, :], pB[:, 0:SCH])
            for m in range(2):
                p.tt(OP.mult, kbcol[g][:, m, :], kcol[g][:, m, :], brep[g][:, :])

        for t in range(SCH):
            for g in range(2):
                p.mm(pR[:, :], kbcol[g][:, :, t], memT[g][:, :])
                p.stt(urow[g][:, :],
                      vrows[g][:, t * 64:(t + 1) * 64],
                      be_sb[g][:, c0 + t: c0 + t + 1],
                      pR[:, :], OP.mult, OP.subtract)
                p.mm(pU[:, :], indic[:, :], urow[g][:, :])
                p.ts(memT[g][:, :], memT[g][:, :],
                     arep[g][:, t:t + 1], OP.mult)
                p.stt(memT[g][:, :], pU[:, :], kstack[g][:, t:t + 1],
                      memT[g][:, :], OP.mult, OP.add)
                p.mm(pO[:, :], qcol[g][:, :, t], memT[g][:, :])
                p.act(orows[g][:, t * 64:(t + 1) * 64], pO[:, :], AF.Copy)

        for g in range(2):
            p.dma(o_out[2 * g:2 * g + 2, c0:c0 + SCH, :], orows[g][:, :])

    for g in range(2):
        p.dma(mem_out[g][:, :], memT[g][:, :])



def build_p9(p):
    D = 1024
    x = p.din("x", [MROWS, D])
    Wq = p.din("Wq", [D, D])
    Wkv = p.din("Wkv", [D, 128])
    gam = p.din("gam", [1, D])
    ident_bf = p.din("identbf", [128, 128], BF16)
    ones1 = p.din("ones1", [1, 128])
    q = p.dout("q", [MROWS, D])
    kv = p.dout("kv", [MROWS, 128])

    idsb = p.sb("idsb", [128, 128], BF16)
    p.dma(idsb[:, :], ident_bf[:, :])
    onesb = p.sb("onesb", [1, 128])
    p.dma(onesb[:, :], ones1[:, :])
    pbank = p.ps("pbank", [128, 512])
    pT = p.ps("pT", [128, 128], BF16)

    Wq_bf = p.sb("Wq_bf", [128, 8 * D], BF16)
    Wkv_bf = p.sb("Wkv_bf", [128, 8 * 128], BF16)
    wtmp = p.sb("wtmp", [128, 512])
    emit_load_wbf(p, Wq, D, D, Wq_bf, wtmp)
    emit_load_wbf(p, Wkv, D, 128, Wkv_bf, wtmp)

    gr = p.sb("gr", [1, D])
    p.dma(gr[:, :], gam[:, :])
    gbc = p.sb("gbc", [128, D])
    emit_bcast_row(p, gr, D, gbc, onesb, pbank)

    xt = p.sb("xt", [128, D])
    xn = p.sb("xn", [128, D])
    xb = p.sb("xb", [128, D], BF16)
    xT = p.sb("xT", [128, D], BF16)
    mu = p.sb("mu", [128, 1])
    var = p.sb("var", [128, 1])
    sq = p.sb("sq", [128, D])
    out_sb = p.sb("out_sb", [128, D])

    for mt in range(MROWS // 128):
        r0 = mt * 128
        p.dma(xt[:, :], x[r0:r0 + 128, :])
        emit_ln(p, xt, D, mu, sq, var, 1e-5, xt, True)
        p.cp(xb[:, :], xt[:, :])
        emit_xT(p, xb, D, xT, idsb, pT)
        emit_matmul_tiles(p, xT, Wkv_bf, D, 128, out_sb, pbank)
        p.dma(kv[r0:r0 + 128, :], out_sb[:, 0:128])
        emit_ln(p, xt, D, mu, sq, var, 1e-5, xn, True)
        p.tt(OP.mult, xn[:, :], xn[:, :], gbc[:, :])
        p.cp(xb[:, :], xn[:, :])
        emit_xT(p, xb, D, xT, idsb, pT)
        emit_matmul_tiles(p, xT, Wq_bf, D, D, out_sb, pbank)
        p.dma(q[r0:r0 + 128, :], out_sb[:, 0:D])



def build_att(cfg):
    nk = cfg["nk"]
    off = cfg["off"]
    l2n = cfg["l2n"]
    neg = cfg["neg"]

    def b(p):
        T = 8192
        qT = p.din("qT", [64, T])
        kT = p.din("kT", [64, nk])
        V = p.din("V", [nk, 64])
        band = p.din("band", [128, 1152])
        ident_bf = p.din("identbf", [128, 128], BF16)
        ones64 = p.din("ones64", [64, 1])
        ones1x64 = p.din("ones1x64", [1, 64])
        if l2n:
            qsc = p.din("qsc", [64, 1])
            ksc = p.din("ksc", [64, 1])
            tf = p.din("tfirst", [128, 512])
        o = p.dout("o", [T, 64])

        idsb = p.sb("idsb", [128, 128], BF16)
        p.dma(idsb[:, :], ident_bf[:, :])
        band_sb = p.sb("band_sb", [128, 1152])
        p.dma(band_sb[:, :], band[:, :])
        o64 = p.sb("o64", [64, 1])
        p.dma(o64[:, :], ones64[:, :])
        o1x64 = p.sb("o1x64", [1, 64])
        p.dma(o1x64[:, :], ones1x64[:, :])
        if l2n:
            tf_sb = p.sb("tf_sb", [128, 512])
            p.dma(tf_sb[:, :], tf[:, :])
            qs_sb = p.sb("qs_sb", [64, 1])
            ks_sb = p.sb("ks_sb", [64, 1])
            p.dma(qs_sb[:, :], qsc[:, :])
            p.dma(ks_sb[:, :], ksc[:, :])

        scratch = p.sb("scratch", [64, max(nk, T)])
        sqt = p.sb("sqt", [64, 512])
        ssr = p.sb("ssr", [1, max(nk, T)])
        pb1 = p.ps("pb1", [1, 512])
        pbc = p.ps("pbc", [64, 512])
        kb = p.sb("kb", [64, nk], BF16)
        qb = p.sb("qb", [64, T], BF16)

        def l2norm_dmaj(src, n, scl_col, extra):
            c0 = 0
            while c0 < n:
                w = min(512, n - c0)
                p.act(sqt[:, 0:w], src[:, c0:c0 + w], AF.Square)
                p.mm(pb1[:, 0:w], o64[:, :], sqt[:, 0:w])
                p.cp(ssr[:, c0:c0 + w], pb1[:, 0:w])
                c0 += w
            p.act(ssr[:, 0:n], ssr[:, 0:n], AF.Sqrt)
            p.ts(ssr[:, 0:n], ssr[:, 0:n], 1e-12, OP.max)
            p.recip(ssr[:, 0:n], ssr[:, 0:n])
            c0 = 0
            while c0 < n:
                w = min(512, n - c0)
                p.mm(pbc[:, 0:w], o1x64[:, :], ssr[:, c0:c0 + w])
                p.tt(OP.mult, src[:, c0:c0 + w], src[:, c0:c0 + w], pbc[:, 0:w])
                c0 += w
            if scl_col is not None:
                p.ts(src[:, 0:n], src[:, 0:n], scl_col, OP.mult,
                     extra, OP.mult)
            elif extra != 1.0:
                p.ts(src[:, 0:n], src[:, 0:n], extra, OP.mult)

        p.dma(scratch[:, 0:nk], kT[:, :])
        if l2n:
            l2norm_dmaj(scratch, nk, ks_sb[:, 0:1], 1.0)
        p.cp(kb[:, :], scratch[:, 0:nk])
        p.dma(scratch[:, 0:T], qT[:, :])
        if l2n:
            l2norm_dmaj(scratch, T, qs_sb[:, 0:1], 8.0)
        else:
            p.ts(scratch[:, 0:T], scratch[:, 0:T], 0.125, OP.mult)
        p.cp(qb[:, :], scratch[:, 0:T])

        ntk_all = (nk + 127) // 128
        Vb = p.sb("Vb", [128, ntk_all * 64], BF16)
        vtmp = p.sb("vtmp", [128, 64])
        for kt in range(ntk_all):
            rw = min(128, nk - kt * 128)
            p.dma(vtmp[0:rw, :], V[kt * 128: kt * 128 + rw, :])
            p.cp(Vb[0:rw, kt * 64:(kt + 1) * 64], vtmp[0:rw, :])

        NCH = (nk + 511) // 512
        S_sb = p.sb("S_sb", [128, NCH * 512])
        mx = p.sb("mx", [128, 1])
        sm = p.sb("sm", [128, 1])
        Pb = p.sb("Pb", [128, NCH * 512], BF16)
        PT = p.sb("PT", [128, 128], BF16)
        ob = p.sb("ob", [128, 64])
        pS = p.ps("pS", [128, 512])
        pPT = p.ps("pPT", [128, 128], BF16)
        pOv = p.ps("pOv", [128, 64])

        for qi in range(T // 128):
            r0 = qi * 128
            kmax = min(nk, r0 + 127 + off + 1)
            nch = (kmax + 511) // 512
            for ch in range(nch):
                c0 = ch * 512
                w = min(512, nk - c0)
                p.mm(pS[:, 0:w], qb[:, r0:r0 + 128], kb[:, c0:c0 + w])
                if l2n and qi == 0 and ch == 0:
                    p.tt(OP.add, S_sb[:, c0:c0 + w], pS[:, 0:w],
                         tf_sb[:, 0:w])
                elif c0 + w - 1 <= r0 + off:
                    p.cp(S_sb[:, c0:c0 + w], pS[:, 0:w])
                else:
                    d = 512 + c0 - r0 - off
                    p.tt(OP.add, S_sb[:, c0:c0 + w], pS[:, 0:w],
                         band_sb[:, d:d + w])
            ncols = nch * 512
            if ncols > nk:
                ncols = ((nk + 127) // 128) * 128 - 0
                ncols = min(nch * 512, nk)
            ncols_r = min(nch * 512, nk)
            p.red(mx[:, :], S_sb[:, 0:ncols_r], OP.max)
            p.ts(mx[:, :], mx[:, :], -1.0, OP.mult)
            p.ts(S_sb[:, 0:ncols_r], S_sb[:, 0:ncols_r], mx[:, 0:1], OP.add)
            p.act(S_sb[:, 0:ncols_r], S_sb[:, 0:ncols_r], AF.Exp)
            p.red(sm[:, :], S_sb[:, 0:ncols_r], OP.add)
            p.recip(sm[:, :], sm[:, :])
            p.ts(S_sb[:, 0:ncols_r], S_sb[:, 0:ncols_r], sm[:, 0:1], OP.mult)
            p.cp(Pb[:, 0:ncols_r], S_sb[:, 0:ncols_r])

            ntk = (ncols_r + 127) // 128
            for kt in range(ntk):
                rw = min(128, ncols_r - kt * 128)
                p.tr(pPT[0:rw, 0:128], Pb[:, kt * 128: kt * 128 + rw], idsb[:, :])
                p.cp(PT[0:rw, :], pPT[0:rw, 0:128])
                p.mm(pOv[:, :], PT[0:rw, :], Vb[0:rw, kt * 64:(kt + 1) * 64],
                     start=(kt == 0), stop=(kt == ntk - 1))
            p.cp(ob[:, :], pOv[:, :])
            p.dma(o[r0:r0 + 128, :], ob[:, :])

    return b



def build_p18(p):
    D = 1024
    x = p.din("x", [MROWS, D])
    wrow = p.din("wrow", [1, D])
    brow = p.din("brow", [1, D])
    Wr = p.din("Wr", [D, 4])
    ident_bf = p.din("identbf", [128, 128], BF16)
    ones1 = p.din("ones1", [1, 128])
    h2 = p.dout("h2", [MROWS, D])
    masks = p.dout("masks", [MROWS, 4])

    idsb = p.sb("idsb", [128, 128], BF16)
    p.dma(idsb[:, :], ident_bf[:, :])
    onesb = p.sb("onesb", [1, 128])
    p.dma(onesb[:, :], ones1[:, :])
    pbank = p.ps("pbank", [128, 512])
    pT = p.ps("pT", [128, 128], BF16)

    wr = p.sb("wr", [1, D])
    br = p.sb("br", [1, D])
    p.dma(wr[:, :], wrow[:, :])
    p.dma(br[:, :], brow[:, :])
    wbc = p.sb("wbc", [128, D])
    bbc = p.sb("bbc", [128, D])
    emit_bcast_row(p, wr, D, wbc, onesb, pbank)
    emit_bcast_row(p, br, D, bbc, onesb, pbank)

    W_bf = p.sb("W_bf", [128, 8 * 4], BF16)
    wtmp = p.sb("wtmp", [128, 512])
    emit_load_wbf(p, Wr, D, 4, W_bf, wtmp)

    xt = p.sb("xt", [128, D])
    xb = p.sb("xb", [128, D], BF16)
    xT = p.sb("xT", [128, D], BF16)
    mu = p.sb("mu", [128, 1])
    var = p.sb("var", [128, 1])
    sq = p.sb("sq", [128, D])
    lg = p.sb("lg", [128, 4])
    mk = p.sb("mk", [128, 4])

    for mt in range(MROWS // 128):
        r0 = mt * 128
        p.dma(xt[:, :], x[r0:r0 + 128, :])
        emit_ln(p, xt, D, mu, sq, var, 1e-5, xt, True)
        p.tt(OP.mult, xt[:, :], xt[:, :], wbc[:, :])
        p.tt(OP.add, xt[:, :], xt[:, :], bbc[:, :])
        p.dma(h2[r0:r0 + 128, :], xt[:, :])
        p.cp(xb[:, :], xt[:, :])
        emit_xT(p, xb, D, xT, idsb, pT)
        emit_matmul_tiles(p, xT, W_bf, D, 4, lg, pbank)
        p.red(mu[:, :], lg[:, 0:4], OP.max)
        p.ts(mk[:, :], lg[:, 0:4], mu[:, 0:1], OP.is_ge)
        p.dma(masks[r0:r0 + 128, :], mk[:, :])


def build_conv3(p):
    D = 1024
    x = p.din("x", [MROWS + 2, D])
    cwr = p.din("cwr", [1, 3 * D])
    ones1 = p.din("ones1", [1, 128])
    y = p.dout("y", [MROWS, D])
    onesb = p.sb("onesb", [1, 128])
    p.dma(onesb[:, :], ones1[:, :])
    pbank = p.ps("pbank", [128, 512])
    cw_sb = p.sb("cw_sb", [1, 3 * D])
    p.dma(cw_sb[:, :], cwr[:, :])
    cwb = p.sb("cwb", [128, 3 * D])
    for j in range(3):
        emit_bcast_row(p, cw_sb[0:1, j * D:(j + 1) * D], D,
                       cwb[:, j * D:(j + 1) * D], onesb, pbank)
    xs = p.sb("xs", [128, D])
    acc = p.sb("acc", [128, D])
    tmp = p.sb("tmp", [128, D])
    for mt in range(MROWS // 128):
        r0 = mt * 128
        for j in range(3):
            p.dma(xs[:, :], x[r0 + j: r0 + j + 128, :])
            if j == 0:
                p.tt(OP.mult, acc[:, :], xs[:, :], cwb[:, 0:D])
            else:
                p.tt(OP.mult, tmp[:, :], xs[:, :], cwb[:, j * D:(j + 1) * D])
                p.tt(OP.add, acc[:, :], acc[:, :], tmp[:, :])
        p.dma(y[r0:r0 + 128, :], acc[:, :])


def build_gate(p):
    H = 4096
    a1 = p.din("a1", [MROWS, H])
    a2 = p.din("a2", [MROWS, H])
    s1 = p.din("s1row", [1, H])
    s2 = p.din("s2row", [1, H])
    ones1 = p.din("ones1", [1, 128])
    y = p.dout("y", [MROWS, H])
    onesb = p.sb("onesb", [1, 128])
    p.dma(onesb, ones1)
    pbank = p.ps("pbank", [128, 512])
    s1_sb = p.sb("s1_sb", [1, H])
    s2_sb = p.sb("s2_sb", [1, H])
    p.dma(s1_sb, s1)
    p.dma(s2_sb, s2)
    s1bc = p.sb("s1bc", [128, H])
    s2bc = p.sb("s2bc", [128, H])
    emit_bcast_row(p, s1_sb, H, s1bc, onesb, pbank)
    emit_bcast_row(p, s2_sb, H, s2bc, onesb, pbank)
    t1 = p.sb("t1", [128, H])
    t2 = p.sb("t2", [128, H])
    t3 = p.sb("t3", [128, H])
    for mt in range(MROWS // 128):
        r0 = mt * 128
        p.dma(t1, a1[r0:r0 + 128, :])
        p.dma(t2, a2[r0:r0 + 128, :])
        p.tt(OP.mult, t1, t1, s1bc)
        p.tt(OP.mult, t2, t2, s2bc)
        p.act(t3, t1, AF.Silu)
        p.tt(OP.mult, t3, t3, t2)
        p.dma(y[r0:r0 + 128, :], t3)
    _ = y


def build_add(p):
    D = 1024
    a = p.din("a", [MROWS, D])
    b_ = p.din("b", [MROWS, D])
    y = p.dout("y", [MROWS, D])
    at = p.sb("at", [128, D])
    bt = p.sb("bt", [128, D])
    for mt in range(MROWS // 128):
        r0 = mt * 128
        p.dma(at[:, :], a[r0:r0 + 128, :])
        p.dma(bt[:, :], b_[r0:r0 + 128, :])
        p.tt(OP.add, at[:, :], at[:, :], bt[:, :])
        p.dma(y[r0:r0 + 128, :], at[:, :])



EYE_BF = np.eye(128).astype(BF)
EYE_F32 = np.eye(128, dtype=np.float32)
ONES1 = np.ones((1, 128), dtype=np.float32)
ONES64 = np.ones((64, 1), dtype=np.float32)
ONES1x64 = np.ones((1, 64), dtype=np.float32)
INDIC = np.zeros((2, 128), dtype=np.float32)
INDIC[0, 0:64] = 1.0
INDIC[1, 64:128] = 1.0


def band_table(off, neg):
    t = np.zeros((128, 1152), dtype=np.float32)
    i = np.arange(128)[:, None]
    m = np.arange(1152)[None, :]
    t[:] = np.where((m - 512) <= i, 0.0, neg)
    return t


def tfirst_table(neg):
    t = np.zeros((128, 512), dtype=np.float32)
    r = np.arange(128)[:, None]
    c = np.arange(512)[None, :]
    causal_bad = c > r + 2
    prefix_free = (r < 64) & (c < 66)
    bad = causal_bad & ~prefix_free
    t[:] = np.where(bad, neg, 0.0)
    return t


def rope_tables():
    T, D = 8192, 64
    inv = 1.0 / (10000.0 ** (np.arange(0, D, 2, dtype=np.float64) / D))
    tt = np.arange(T, dtype=np.float64)
    fr = np.outer(tt, inv)
    emb = np.concatenate([fr, fr], axis=-1)
    return np.cos(emb).astype(np.float32), np.sin(emb).astype(np.float32)


def shard_rows(arr):
    return [np.ascontiguousarray(arr[c * MROWS:(c + 1) * MROWS]) for c in range(NC)]


def tok_launch(key, builder, per_core_named, shared_named, outnames):
    inmaps = []
    for c in range(NC):
        m = {}
        for nm, lst in per_core_named.items():
            m[nm] = np.ascontiguousarray(lst[c])
        for nm, v in shared_named.items():
            m[nm] = np.ascontiguousarray(v)
        inmaps.append(m)
    return launch(key, builder, inmaps, outnames)


def gather_rows(res, idx=0):
    return np.concatenate([r[idx] for r in res], axis=0)


def kernel(**I):
    f32 = np.float32
    x0 = np.ascontiguousarray(I["L0_attn_x"].astype(f32).reshape(16384, 1024))

    lin_cfgs = {
        "p1a": dict(K=1024, N=4640, pro="ln", epi="none"),
        "p3": dict(K=1024, N=1024, pro="rms_w", epi="res"),
        "p4": dict(K=1024, N=4096, pro="ln", epi="none"),
        "p6": dict(K=4096, N=2048, pro="none", epi="scale_gelu"),
        "p8": dict(K=4096, N=1024, pro="none", epi="scale_res"),
        "p11": dict(K=1024, N=1024, pro="none", epi="res"),
        "p15": dict(K=1024, N=3072, pro="none", epi="rope"),
        "p19": dict(K=1024, N=4096, pro="none", epi="none"),
        "p20": dict(K=4096, N=1024, pro="none", epi="mask_acc"),
    }

    def lin(key, x_arr, W, outnames=("y",), **extra):
        cfg = lin_cfgs[key]
        per = {"x": shard_rows(x_arr)}
        shared = {"W": W.astype(f32), "identbf": EYE_BF, "ones1": ONES1}
        for nm, v in extra.items():
            if isinstance(v, list):
                per[nm] = v
            else:
                shared[nm] = np.ascontiguousarray(v.astype(f32))
        res = tok_launch("lin_" + key, build_linear(cfg), per, shared, list(outnames))
        return gather_rows(res)

    Wcat = np.concatenate([I["L0_attn_Wq"], I["L0_attn_Wk"], I["L0_attn_Wv"],
                           I["L0_attn_Wrq"], I["L0_attn_Wrk"],
                           I["L0_attn_Wf"], I["L0_attn_Wl"]], axis=1).astype(f32)
    qkvr = lin("p1a", x0, Wcat)

    halo = []
    for c in range(NC):
        lo = c * MROWS
        h = np.zeros((MROWS + 3, 4640), f32)
        h[3:] = qkvr[lo:lo + MROWS]
        nprev = 0 if lo % 8192 == 0 else 3
        if nprev:
            h[3 - nprev:3] = qkvr[lo - nprev:lo]
        halo.append(h)
    cwr = np.stack([np.concatenate([I["L0_attn_conv_q"][:, j],
                                    I["L0_attn_conv_k"][:, j],
                                    I["L0_attn_conv_v"][:, j]]) for j in range(4)]
                   ).astype(f32)
    res = tok_launch("p1b", build_p1b,
                     {"x": halo},
                     {"cwr": cwr.reshape(1, -1),
                      "qn": I["L0_attn_qn_w"].astype(f32).reshape(1, 64),
                      "kn": I["L0_attn_kn_w"].astype(f32).reshape(1, 64),
                      "gs": I["L0_attn_graph_scale"].astype(f32).reshape(1, 16),
                      "bfr": I["L0_attn_bf"].astype(f32).reshape(1, 16),
                      "blr": I["L0_attn_bl"].astype(f32).reshape(1, 16),
                      "ones1": ONES1},
                     ["ktil", "qtil", "vout", "aout", "bout"])
    ktil = gather_rows(res, 0)
    qtil = gather_rows(res, 1)
    vv = gather_rows(res, 2)
    alp = gather_rows(res, 3)
    bet = gather_rows(res, 4)

    def to_bh(a):
        return np.ascontiguousarray(
            a.reshape(2, 8192, 16, 64).transpose(0, 2, 1, 3).reshape(32, 8192, 64))

    k_bh = to_bh(ktil)
    q_bh = to_bh(qtil)
    v_bh = to_bh(vv)
    al_bh = np.ascontiguousarray(alp.reshape(2, 8192, 16).transpose(0, 2, 1).reshape(32, 8192))
    be_bh = np.ascontiguousarray(bet.reshape(2, 8192, 16).transpose(0, 2, 1).reshape(32, 8192))

    mem = [np.zeros((2, 128, 64), f32) for _ in range(NC)]
    o_bh = np.zeros((32, 8192, 64), f32)
    for s in range(8192 // SEG):
        t0 = s * SEG
        inmaps = []
        for c in range(NC):
            u = slice(4 * c, 4 * c + 4)
            inmaps.append({
                "k": np.ascontiguousarray(k_bh[u, t0:t0 + SEG, :]),
                "v": np.ascontiguousarray(v_bh[u, t0:t0 + SEG, :]),
                "q": np.ascontiguousarray(q_bh[u, t0:t0 + SEG, :]),
                "al0": np.ascontiguousarray(al_bh[4 * c:4 * c + 2, t0:t0 + SEG]),
                "al1": np.ascontiguousarray(al_bh[4 * c + 2:4 * c + 4, t0:t0 + SEG]),
                "be0": np.ascontiguousarray(be_bh[4 * c:4 * c + 2, t0:t0 + SEG]),
                "be1": np.ascontiguousarray(be_bh[4 * c + 2:4 * c + 4, t0:t0 + SEG]),
                "mem0": mem[c][0], "mem1": mem[c][1],
                "indic": INDIC, "ident": EYE_F32,
            })
        res = launch("scan", build_scan, inmaps, ["o", "memo0", "memo1"])
        for c in range(NC):
            o_bh[4 * c:4 * c + 4, t0:t0 + SEG, :] = res[c][0]
            mem[c] = [res[c][1], res[c][2]]

    o_tok = np.ascontiguousarray(
        o_bh.reshape(2, 16, 8192, 64).transpose(0, 2, 1, 3).reshape(16384, 1024))
    x1 = lin("p3", o_tok, I["L0_attn_Wo"],
             prow=I["L0_attn_on_w"].reshape(1, 1024), res=shard_rows(x0))

    A1 = lin("p4", x1, np.ascontiguousarray(I["L0_ffn_w1_w"].T))
    A2 = lin("p4", x1, np.ascontiguousarray(I["L0_ffn_w2_w"].T))
    res = tok_launch("gate", build_gate,
                     {"a1": shard_rows(A1), "a2": shard_rows(A2)},
                     {"s1row": I["L0_ffn_w1_s"].reshape(1, 4096).astype(f32),
                      "s2row": I["L0_ffn_w2_s"].reshape(1, 4096).astype(f32),
                      "ones1": ONES1}, ["y"])
    h = gather_rows(res)
    H1 = I["L0_ffn_h1_w"].T.astype(f32)
    g1 = np.concatenate(
        [lin("p6", h, np.ascontiguousarray(H1[:, j * 2048:(j + 1) * 2048]),
             srow=I["L0_ffn_h1_s"][j * 2048:(j + 1) * 2048].reshape(1, 2048))
         for j in range(2)], axis=1)
    H2 = I["L0_ffn_h2_w"].T.astype(f32)
    g2 = np.concatenate(
        [lin("p6", g1, np.ascontiguousarray(H2[:, j * 2048:(j + 1) * 2048]),
             srow=I["L0_ffn_h2_s"][j * 2048:(j + 1) * 2048].reshape(1, 2048))
         for j in range(2)], axis=1)
    x2 = lin("p8", g2, I["L0_ffn_w3_w"].T,
             srow=I["L0_ffn_w3_s"].reshape(1, 1024), res=shard_rows(x1))

    res = tok_launch("p9", build_p9, {"x": shard_rows(x2)},
                     {"Wq": I["L1_attn_Wq"].astype(f32),
                      "Wkv": I["L1_attn_Wkv"].astype(f32),
                      "gam": I["L1_attn_gamma"].reshape(1, 1024),
                      "identbf": EYE_BF, "ones1": ONES1},
                     ["q", "kv"])
    qm = gather_rows(res, 0)
    kvm = gather_rows(res, 1)
    qT_u = np.ascontiguousarray(
        qm.reshape(2, 8192, 16, 64).transpose(0, 2, 3, 1).reshape(32, 64, 8192))
    k_b = kvm.reshape(2, 8192, 128)[:, :, 0:64]
    v_b = kvm.reshape(2, 8192, 128)[:, :, 64:128]
    kT_b = np.zeros((2, 64, 8194), f32)
    V_b = np.zeros((2, 8194, 64), f32)
    nk0 = I["L1_attn_null_kv"][0].astype(f32)
    nv0 = I["L1_attn_null_kv"][1].astype(f32)
    for b in range(2):
        kT_b[b, :, 0:2] = nk0.T
        kT_b[b, :, 2:] = k_b[b].T
        V_b[b, 0:2] = nv0
        V_b[b, 2:] = v_b[b]
    band10 = band_table(2, NEGBIG)
    tf10 = tfirst_table(NEGBIG)
    o_mqa = np.zeros((32, 8192, 64), f32)
    att10 = build_att(dict(nk=8194, off=2, l2n=True, neg=NEGBIG))
    for j in range(4):
        inmaps = []
        units = []
        for c in range(NC):
            u = 4 * c + j
            units.append(u)
            b = u // 16
            inmaps.append({"qT": np.ascontiguousarray(qT_u[u]),
                           "kT": np.ascontiguousarray(kT_b[b]),
                           "V": np.ascontiguousarray(V_b[b]),
                           "band": band10, "identbf": EYE_BF,
                           "ones64": ONES64, "ones1x64": ONES1x64,
                           "qsc": I["L1_attn_q_scale"].reshape(64, 1).astype(f32),
                           "ksc": I["L1_attn_k_scale"].reshape(64, 1).astype(f32),
                           "tfirst": tf10})
        res = launch("att10", att10, inmaps, ["o"])
        for c, u in enumerate(units):
            o_mqa[u] = res[c][0]
    o_tok2 = np.ascontiguousarray(
        o_mqa.reshape(2, 16, 8192, 64).transpose(0, 2, 1, 3).reshape(16384, 1024))
    x3 = lin("p11", o_tok2, I["L1_attn_Wout"], res=shard_rows(x2))

    res = tok_launch("ln_xin", build_ln_sb, {"x": shard_rows(x3)},
                     {"wrow": np.ones((1, 1024), f32),
                      "brow": np.zeros((1, 1024), f32), "ones1": ONES1}, ["y"])
    xin = gather_rows(res)
    res = tok_launch("ln_xin", build_ln_sb, {"x": shard_rows(xin)},
                     {"wrow": I["L1_ffn_ln1_w"].reshape(1, 1024).astype(f32),
                      "brow": I["L1_ffn_ln1_b"].reshape(1, 1024).astype(f32),
                      "ones1": ONES1}, ["y"])
    hn = gather_rows(res)
    halo2 = []
    for c in range(NC):
        lo = c * MROWS
        hh = np.zeros((MROWS + 2, 1024), f32)
        hh[2:] = hn[lo:lo + MROWS]
        nprev = 0 if lo % 8192 == 0 else 2
        if nprev:
            hh[2 - nprev:2] = hn[lo - nprev:lo]
        halo2.append(hh)
    res = tok_launch("conv3", build_conv3, {"x": halo2},
                     {"cwr": np.ascontiguousarray(I["L1_ffn_conv_w"][:, 0, :].T.astype(f32)).reshape(1, -1),
                      "ones1": ONES1}, ["y"])
    hc = gather_rows(res)
    cos_t, sin_t = rope_tables()
    cos_full = np.tile(cos_t, (2, 1))
    sin_full = np.tile(sin_t, (2, 1))
    qkv = lin("p15", hc, I["L1_ffn_attn_w"],
              **{"cost": shard_rows(cos_full), "sint": shard_rows(sin_full)})
    qr = qkv[:, 0:1024]
    kr = qkv[:, 1024:2048]
    vr = qkv[:, 2048:3072]
    qT2 = np.ascontiguousarray(
        qr.reshape(2, 8192, 16, 64).transpose(0, 2, 3, 1).reshape(32, 64, 8192))
    kT2 = np.ascontiguousarray(
        kr.reshape(2, 8192, 16, 64).transpose(0, 2, 3, 1).reshape(32, 64, 8192))
    V2 = np.ascontiguousarray(
        vr.reshape(2, 8192, 16, 64).transpose(0, 2, 1, 3).reshape(32, 8192, 64))
    band16 = band_table(0, -1e30)
    att16 = build_att(dict(nk=8192, off=0, l2n=False, neg=-1e30))
    o_att2 = np.zeros((32, 8192, 64), f32)
    for j in range(4):
        inmaps = []
        units = []
        for c in range(NC):
            u = 4 * c + j
            units.append(u)
            inmaps.append({"qT": qT2[u], "kT": kT2[u], "V": V2[u],
                           "band": band16, "identbf": EYE_BF,
                           "ones64": ONES64, "ones1x64": ONES1x64})
        res = launch("att16", att16, inmaps, ["o"])
        for c, u in enumerate(units):
            o_att2[u] = res[c][0]
    y_tok = np.ascontiguousarray(
        o_att2.reshape(2, 16, 8192, 64).transpose(0, 2, 1, 3).reshape(16384, 1024))
    y1 = lin("p11", y_tok, I["L1_ffn_proj_w"], res=shard_rows(xin))

    res = tok_launch("p18", build_p18, {"x": shard_rows(y1)},
                     {"wrow": I["L1_ffn_ln2_w"].reshape(1, 1024).astype(f32),
                      "brow": I["L1_ffn_ln2_b"].reshape(1, 1024).astype(f32),
                      "Wr": I["L1_ffn_router_w"].astype(f32),
                      "identbf": EYE_BF, "ones1": ONES1},
                     ["h2", "masks"])
    h2 = gather_rows(res, 0)
    masks = gather_rows(res, 1)

    res = tok_launch("add", build_add,
                     {"a": shard_rows(x3), "b": shard_rows(y1)}, {}, ["y"])
    acc = gather_rows(res)
    ones4k = np.ones((1, 4096), f32)
    for e in range(4):
        B1 = lin("p19", h2, np.ascontiguousarray(I["L1_ffn_expert_wv"][e][:, 0:4096]))
        B2 = lin("p19", h2, np.ascontiguousarray(I["L1_ffn_expert_wv"][e][:, 4096:8192]))
        res = tok_launch("gate", build_gate,
                         {"a1": shard_rows(B1), "a2": shard_rows(B2)},
                         {"s1row": ones4k, "s2row": ones4k, "ones1": ONES1}, ["y"])
        ge = gather_rows(res)
        acc = lin("p20", ge, I["L1_ffn_expert_proj"][e],
                  res=shard_rows(acc),
                  mcol=shard_rows(np.ascontiguousarray(masks[:, e:e + 1])))

    return acc.reshape(2, 8192, 1024).astype(np.float32)
